# version 05
# baseline (speedup 1.0000x reference)
import math
import numpy as np

import concourse.bass as bass
import concourse.mybir as mybir
from concourse.bass_utils import run_bass_kernel_spmd

F32 = mybir.dt.float32
BF16 = mybir.dt.bfloat16
U32 = mybir.dt.uint32
AF = mybir.ActivationFunctionType
ALU = mybir.AluOpType
AX = mybir.AxisListType.X

D = 2048
HID = 8192
L = 8192
HLF = 4096
S = 1024
NH = 16
DH = 128
NCORE = 8
INV_SCALE = 1.0 / math.sqrt(DH)
N_CH = HLF // 128
N_CS = S // 128
KC = D // 128
HC = HID // 512
EPS = 1e-5

CHAIN = {"te": "pe", "sc": "act", "ve": "dve", "gp": "plc", "sy": "syc"}


class Buf:
    __slots__ = ("name", "lo", "hi", "ready", "readers")

    def __init__(self, name, lo=None, hi=None):
        self.name = name
        self.lo = lo
        self.hi = hi
        self.ready = None
        self.readers = []


class Sched:
    def __init__(self, sem_names):
        self.cnt = {n: 0 for n in sem_names}
        self.ops = {e: [] for e in CHAIN}
        self.wm = {e: {n: 0 for n in sem_names} for e in CHAIN}
        self.sbufs = []

    def reg(self, buf):
        self.sbufs.append(buf)
        return buf

    def inc(self, name, n=1):
        self.cnt[name] += n
        return (name, self.cnt[name])

    def _waits(self, ek, toks):
        out = []
        for t in toks:
            if t is None:
                continue
            name, val = t
            assert val <= self.cnt[name], (name, val, self.cnt[name])
            if self.wm[ek][name] >= val:
                continue
            self.wm[ek][name] = val
            out.append((name, val))
        return out

    def op(self, ek, fn, r=(), w=(), sem=None, n=1):
        toks = []
        for b in r:
            toks.append(b.ready)
        for b in w:
            toks.append(b.ready)
            toks.extend(b.readers)
            if b.lo is not None:
                for o in self.sbufs:
                    if o is b or o.lo is None:
                        continue
                    if o.lo < b.hi and b.lo < o.hi:
                        toks.append(o.ready)
                        toks.extend(o.readers)
                        if b.lo <= o.lo and b.hi >= o.hi:
                            o.ready = None
                            o.readers = []
                        else:
                            pass
        waits = self._waits(ek, toks)
        if sem is None:
            sem = CHAIN[ek]
        tok = self.inc(sem, n)
        self.ops[ek].append((waits, fn, sem, n))
        for b in r:
            b.readers.append(tok)
        for b in w:
            b.ready = tok
            b.readers = []
        return tok

    def raw(self, ek, fn, waits=(), incs=()):
        w = self._waits(ek, [t for t in waits])
        for (s_, n_) in incs:
            self.inc(s_, n_)
        self.ops[ek].append((w, fn, None, 0))


def _build_program(debug=False):
    nc = bass.Bass("TRN2", target_bir_lowering=False, num_devices=NCORE)

    xh = nc.dram_tensor("xh", [L, D], F32, kind="ExternalInput").ap()
    wq_d = nc.dram_tensor("wq", [D, D], F32, kind="ExternalInput").ap()
    wk_d = nc.dram_tensor("wk", [D, D], F32, kind="ExternalInput").ap()
    wv_d = nc.dram_tensor("wv", [D, D], F32, kind="ExternalInput").ap()
    wu_d = nc.dram_tensor("wu", [D, D], F32, kind="ExternalInput").ap()
    wo_d = nc.dram_tensor("wo", [D, D], F32, kind="ExternalInput").ap()
    wgf = nc.dram_tensor("wgf", [HID, D], F32, kind="ExternalInput").ap()
    wuf = nc.dram_tensor("wuf", [HID, D], F32, kind="ExternalInput").ap()
    wdf = nc.dram_tensor("wdf", [D, HID], F32, kind="ExternalInput").ap()
    out_d = nc.dram_tensor("out", [S, D], F32, kind="ExternalOutput").ap()
    dbg = {}
    if debug:
        dbg["kv"] = nc.dram_tensor("dbg_kv", [128, D], F32, kind="ExternalOutput").ap()
        dbg["xres"] = nc.dram_tensor("dbg_xres", [S, D], F32, kind="ExternalOutput").ap()
        dbg["h0"] = nc.dram_tensor("dbg_h0", [128, D], F32, kind="ExternalOutput").ap()
        dbg["wg0"] = nc.dram_tensor("dbg_wg0", [128, D], F32, kind="ExternalOutput").ap()
        dbg["gat0"] = nc.dram_tensor("dbg_gat0", [128, D], F32, kind="ExternalOutput").ap()
        dbg["h2f0"] = nc.dram_tensor("dbg_h2f0", [128, D], F32, kind="ExternalOutput").ap()
        dbg["kl"] = nc.dram_tensor("dbg_kl", [128, D], F32, kind="ExternalOutput").ap()
        dbg["vl"] = nc.dram_tensor("dbg_vl", [128, D], F32, kind="ExternalOutput").ap()


    stg_g = nc.dram_tensor("stg_g", [HID, D], BF16)
    stg_u = nc.dram_tensor("stg_u", [HID, D], BF16)
    stg_d = nc.dram_tensor("stg_d", [D, HID], BF16)
    xres_d = nc.dram_tensor("xres_d", [S, D], F32)
    h2_d = nc.dram_tensor("h2_d", [KC, 128, S], BF16)
    hfs_d = nc.dram_tensor("hfs_d", [KC, 128, S], BF16)

    pid = nc.partition_id_tensor

    ARENA = 204800
    arena = nc.alloc_sbuf_tensor("arena", [128, ARENA], mybir.dt.uint8).ap()

    sem_names = ["qs", "qp", "qa", "pe", "act", "dve", "plc", "syc", "po", "st"]
    k = Sched(sem_names)

    def view(name, off, dtype, *shape):
        es = {F32: 4, BF16: 2, U32: 4}[dtype]
        tot = 1
        for s_ in shape:
            tot *= s_
        assert off % 64 == 0 and off + tot * es <= ARENA, (name, off, shape)
        ap = arena.bitcast(dtype)[:, off // es: off // es + tot]
        if len(shape) == 2:
            ap = ap.rearrange("p (a b) -> p a b", b=shape[1])
        elif len(shape) == 3:
            ap = ap.rearrange("p (a b c) -> p a b c", b=shape[1], c=shape[2])
        buf = k.reg(Buf(name, off, off + tot * es))
        return ap, buf

    IDN, b_idn = view("idn", 0, BF16, 128)
    v_pid, b_pid = view("pid", 512, U32, 1)
    v_non, b_non = view("non", 576, U32, 8)
    v_fl, b_fl = view("fl", 640, U32, 2)
    v_msum, b_msum = view("msum", 704, F32, 1)
    v_m, b_m = view("m", 768, F32, 1)
    v_msq, b_msq = view("msq", 832, F32, 1)
    v_var, b_var = view("var", 896, F32, 1)
    v_rstd1, b_rstd1 = view("rstd1", 960, F32, 1)
    v_ssq, b_ssq = view("ssq", 1024, F32, 1)
    v_ss, b_ss = view("ss", 1088, F32, 16)
    v_rstd, b_rstd = view("rstd", 1152, F32, 16)
    v_ones, b_ones = view("ones", 1280, F32, 128)
    SCR0, SCR1 = 2048, 2048 + 8192
    v_scr0, b_scr0 = view("scr0", SCR0, F32, 2048)
    v_scr1, b_scr1 = view("scr1", SCR1, F32, 2048)
    v_scr0b, b_scr0b = view("scr0b", SCR0, BF16, 2048)
    v_scr1b, b_scr1b = view("scr1b", SCR1, BF16, 2048)
    XIO0 = SCR1 + 8192
    XIO1 = XIO0 + 8192
    v_xio0, b_xio0 = view("xio0", XIO0, F32, 2048)
    v_xio1, b_xio1 = view("xio1", XIO1, F32, 2048)
    KVA = XIO1 + 8192
    v_kvsb, b_kvsb = view("kvsb", KVA, F32, 2048)
    v_kvot, b_kvot = view("kvot", KVA + 8192, F32, 2048)
    v_kvbf, b_kvbf = view("kvbf", KVA + 16384, BF16, 2048)
    A1B = KVA + 20480
    v_wk, b_wk = view("wk", A1B, BF16, 16, 2048)
    v_wv, b_wv = view("wv", A1B + 65536, BF16, 16, 2048)
    HCH0 = A1B + 131072
    v_hchA, b_hchA = view("hchA", HCH0, BF16, 16, 128)
    v_hchB, b_hchB = view("hchB", HCH0 + 4096, BF16, 16, 128)
    v_kt, b_kt = view("kt", HCH0 + 8192, BF16, 2048)
    v_vt, b_vt = view("vt", HCH0 + 12288, BF16, 2048)
    assert HCH0 + 16384 <= ARENA
    v_qfm, b_qfm = view("qfm", A1B, BF16, 16, 1024)
    v_ut, b_ut = view("ut", A1B + 32768, BF16, 8, 2048)
    WOB = A1B + 65536
    v_hfs, b_hfs = view("hfs", WOB, BF16, 16, 1024)
    v_wo, b_wo = view("wo", WOB, BF16, 16, 2048)
    WSTR = WOB + 32768
    v_wstr0, b_wstr0 = view("wstr0", WSTR, BF16, 16, 512)
    v_wstr1, b_wstr1 = view("wstr1", WSTR + 16384, BF16, 16, 512)
    SUF = WOB + 65536
    v_suf, b_suf = view("suf", SUF, BF16, 16, 128)
    v_h2c, b_h2c = view("h2c", SUF + 4096, BF16, 16, 128)
    v_acc, b_acc = view("acc", A1B, F32, 8, 2048)
    v_h2f, b_h2f = view("h2f", A1B + 65536, BF16, 16, 1024)
    FW = A1B + 98304
    v_wgT, b_wgT = view("wgT", FW, BF16, 16, 512)
    v_wuT, b_wuT = view("wuT", FW + 16384, BF16, 16, 512)
    v_wdT, b_wdT = view("wdT", FW + 32768, BF16, 4, 2048)
    v_gat, b_gat = view("gat", KVA, BF16, 4, 1024)
    v_sg, b_sg = view("sg", KVA + 8192, BF16, 512)
    v_sg2, b_sg2 = view("sg2", KVA + 9216, BF16, 512)
    assert FW + 49152 <= ARENA

    ps = nc.alloc_psum_tensor("ps", [128, 4096], F32).ap()

    def bank(i):
        return ps[:, i * 512:(i + 1) * 512]

    psb = [Buf(f"psb{i}") for i in range(8)]

    b_staged = Buf("staged")
    b_r0 = Buf("r0")
    b_xres_d = Buf("xres_d")
    b_h2_d = Buf("h2_d")
    b_hfs_d = Buf("hfs_d")
    x_scrbufs = [b_scr0, b_scr1]

    def matmul_group(reads, banks, parts):
        nparts = len(parts)
        for i, (o, l_, r_, st_, sp_) in enumerate(parts):
            fn = (lambda oo, ll, rr, ss, pp: (lambda e, H: e.matmul(
                oo, ll, rr, start=ss, stop=pp, skip_group_check=True)))(o, l_, r_, st_, sp_)
            if i == 0 and i == nparts - 1:
                tok = k.op("te", fn, r=reads, w=banks)
            elif i == 0:
                toks = [b.ready for b in reads]
                for b in banks:
                    toks.append(b.ready)
                    toks.extend(b.readers)
                waits = k._waits("te", toks)
                k.ops["te"].append((waits, fn, "pe", 0))
            elif i == nparts - 1:
                tok = k.inc("pe", 1)
                k.ops["te"].append(((), fn, "pe", 1))
            else:
                k.ops["te"].append(((), fn, "pe", 0))
        for b in reads:
            b.readers.append(tok)
        for b in banks:
            b.ready = tok
            b.readers = []
        return tok

    def ln_tile(xv, xbuf, scrb_ap, scrb_buf):
        k.op("sc", (lambda sb, xx: (lambda e, H: e.activation(
            sb, xx, AF.Square, accum_out=v_ssq[0:128, 0:1])))(scrb_ap, xv),
            r=(xbuf,), w=(scrb_buf, b_ssq))
        k.op("ve", (lambda xx: (lambda e, H: e.reduce_sum(v_msum, xx, AX)))(xv),
             r=(xbuf,), w=(b_msum,))
        k.op("ve", lambda e, H: e.tensor_scalar_mul(v_m, v_msum, 1.0 / D),
             r=(b_msum,), w=(b_m,))
        k.op("ve", lambda e, H: e.tensor_mul(v_msq, v_m, v_m),
             r=(b_m,), w=(b_msq,))
        k.op("ve", lambda e, H: e.tensor_scalar(v_var, v_ssq, 1.0 / D,
                                                v_msq[0:128, 0:1],
                                                ALU.mult, ALU.subtract),
             r=(b_ssq, b_msq), w=(b_var,))
        k.op("ve", lambda e, H: e.tensor_scalar_add(v_var, v_var, EPS),
             r=(b_var,), w=(b_var,))
        k.op("sc", lambda e, H: e.activation(v_rstd1, v_var, AF.Ln),
             r=(b_var,), w=(b_rstd1,))
        k.op("sc", lambda e, H: e.activation(v_rstd1, v_rstd1, AF.Exp, scale=-0.5),
             r=(b_rstd1,), w=(b_rstd1,))
        k.op("ve", (lambda sb, xx: (lambda e, H: e.tensor_scalar(
            sb, xx, v_m[0:128, 0:1], v_rstd1[0:128, 0:1],
            ALU.subtract, ALU.mult)))(scrb_ap, xv),
            r=(xbuf, b_m, b_rstd1), w=(scrb_buf,))

    def transpose_2048(src_bf, src_buf, dst_ap_fn, dst_buf, banks=(2, 3)):
        for grp in range(4):
            bi = banks[grp % 2]
            parts = []
            for j in range(4):
                kc = grp * 4 + j
                parts.append((bank(bi)[:, j * 128:(j + 1) * 128],
                              src_bf[:, kc * 128:(kc + 1) * 128], IDN,
                              True, True))
            matmul_group((src_buf, b_idn), (psb[bi],), parts)
            for j in range(4):
                kc = grp * 4 + j
                k.op("ve", (lambda o, i_: (lambda e, H: e.tensor_copy(o, i_)))(
                    dst_ap_fn(kc), bank(bi)[:, j * 128:(j + 1) * 128]),
                    r=(psb[bi],), w=(dst_buf,))

    k.op("gp", lambda e, H: e.memset(v_ones, 1.0), w=(b_ones,))
    k.op("gp", lambda e, H: e.affine_select(IDN, v_ones, [[1, 128]],
                                            ALU.is_equal, 0.0, base=0,
                                            channel_multiplier=-1),
         r=(b_ones,), w=(b_idn,))
    for src, dstv, dbuf in ((wk_d, v_wk, b_wk), (wv_d, v_wv, b_wv)):
        for c in range(4):
            k.op("gp", (lambda dv, sr, cc: (lambda e, H: e.dma_start(
                out=dv[:, :, cc * 512:(cc + 1) * 512],
                in_=sr[:, cc * 512:(cc + 1) * 512]
                .rearrange("(kc p) n -> p kc n", p=128))))(dstv, src, c),
                w=(dbuf,), sem="qp", n=16)

    for tsrc, tdst, nchunk, rows in ((wgf, stg_g, 16, 512), (wuf, stg_u, 16, 512),
                                     (wdf, stg_d, 8, 256)):
        for c in range(nchunk):
            r0, r1 = c * rows, (c + 1) * rows
            k.op("gp", (lambda a, b2, rr0, rr1: (lambda e, H: e.dma_start(
                out=b2[rr0:rr1, :], in_=a[rr0:rr1, :])))(tsrc, tdst, r0, r1),
                w=(b_staged,), sem="qp", n=16)

    NT_ALL = L // 128

    def emit_ln(t):
        j = t % 2
        xb = b_xio0 if j == 0 else b_xio1
        xv = v_xio0 if j == 0 else v_xio1
        k.op("sy", (lambda vv, tt: (lambda e, H: e.dma_start(
            out=vv, in_=xh[tt * 128:(tt + 1) * 128, :])))(xv, t),
            w=(xb,), sem="qs", n=16)
        scrb_ap = v_scr0b if j == 0 else v_scr1b
        scrb_buf = b_scr0b if j == 0 else b_scr1b
        ln_tile(xv, xb, scrb_ap, scrb_buf)

    emit_ln(0)
    for t in range(NT_ALL):
        j = t % 2
        if t + 1 < NT_ALL:
            emit_ln(t + 1)
        scrb_ap = v_scr0b if j == 0 else v_scr1b
        scrb_buf = b_scr0b if j == 0 else b_scr1b
        hch_ap = v_hchA if j == 0 else v_hchB
        hch_buf = b_hchA if j == 0 else b_hchB
        transpose_2048(scrb_ap, scrb_buf,
                       (lambda ha: (lambda kc: ha[:, kc, :]))(hch_ap), hch_buf,
                       banks=(2, 3))
        if t < N_CS:
            k.op("sy", (lambda ha, tt: (lambda e, H: e.dma_start(
                out=hfs_d[:, :, tt * 128:(tt + 1) * 128].rearrange("a p b -> p a b"),
                in_=ha[:, :, :])))(hch_ap, t),
                r=(hch_buf,), w=(b_hfs_d,), sem="qs", n=16)
        parts = []
        for kc in range(KC):
            for c in range(4):
                parts.append((bank(c), hch_ap[:, kc, :],
                              v_wk[:, kc, c * 512:(c + 1) * 512],
                              kc == 0, kc == KC - 1))
            for c in range(4):
                parts.append((bank(4 + c), hch_ap[:, kc, :],
                              v_wv[:, kc, c * 512:(c + 1) * 512],
                              kc == 0, kc == KC - 1))
        matmul_group((hch_buf, b_wk, b_wv), tuple(psb[i] for i in range(8)), parts)
        for c in range(4):
            k.op("sc", (lambda cc: (lambda e, H: e.activation(
                v_kt[:, cc * 512:(cc + 1) * 512], bank(cc), AF.Relu,
                scale=INV_SCALE)))(c),
                r=(psb[c],), w=(b_kt,))
        for c in range(4):
            k.op("ve", (lambda cc: (lambda e, H: e.tensor_copy(
                v_vt[:, cc * 512:(cc + 1) * 512], bank(4 + cc))))(c),
                r=(psb[4 + c],), w=(b_vt,))
        if debug and t == 0:
            k.op("ve", (lambda ha: (lambda e, H: e.tensor_copy(
                v_scr0, ha.rearrange("p a b -> p (a b)"))))(hch_ap),
                r=(hch_buf,), w=(b_scr0,))
            k.op("sy", lambda e, H: e.dma_start(out=dbg["h0"], in_=v_scr0),
                 r=(b_scr0,), sem="st", n=16)
        if debug and t == NT_ALL - 1:
            k.op("ve", lambda e, H: e.tensor_copy(v_scr0, v_kt),
                 r=(b_kt,), w=(b_scr0,))
            k.op("sy", lambda e, H: e.dma_start(out=dbg["kl"], in_=v_scr0),
                 r=(b_scr0,), sem="st", n=16)
            k.op("ve", lambda e, H: e.tensor_copy(v_scr1, v_vt),
                 r=(b_vt,), w=(b_scr1,))
            k.op("sy", lambda e, H: e.dma_start(out=dbg["vl"], in_=v_scr1),
                 r=(b_scr1,), sem="st", n=16)
        parts = []
        for h in range(NH):
            bi = 4 + h // 4
            col = (h % 4) * 128
            parts.append((ps[:, bi * 512 + col: bi * 512 + col + 128],
                          v_kt[:, h * 128:(h + 1) * 128],
                          v_vt[:, h * 128:(h + 1) * 128],
                          True, True))
        matmul_group((b_kt, b_vt), (psb[4], psb[5], psb[6], psb[7]), parts)
        for c in range(4):
            if t == 0:
                k.op("ve", (lambda cc: (lambda e, H: e.tensor_copy(
                    v_kvsb[:, cc * 512:(cc + 1) * 512], bank(4 + cc))))(c),
                    r=(psb[4 + c],), w=(b_kvsb,))
            else:
                k.op("ve", (lambda cc: (lambda e, H: e.tensor_add(
                    v_kvsb[:, cc * 512:(cc + 1) * 512],
                    v_kvsb[:, cc * 512:(cc + 1) * 512],
                    bank(4 + cc))))(c),
                    r=(psb[4 + c],), w=(b_kvsb,))

    k.op("ve", lambda e, H: e.tensor_copy(v_kvbf, v_kvsb),
         r=(b_kvsb,), w=(b_kvbf,))
    if debug:
        k.op("sy", lambda e, H: e.dma_start(out=dbg["kv"], in_=v_kvsb),
             r=(b_kvsb,), sem="st", n=16)

    k.op("sy", lambda e, H: e.dma_start(
        out=v_hfs[:, :, :], in_=hfs_d.rearrange("a p b -> p a b")),
        r=(b_hfs_d,), w=(b_hfs,), sem="qs", n=16)

    wstrv = [v_wstr0, v_wstr1]
    wstrb = [b_wstr0, b_wstr1]
    for wsrc, which in ((wq_d, "q"), (wu_d, "u")):
        for c in range(4):
            wb = wstrb[c % 2]
            k.op("gp", (lambda sr, cc, jj: (lambda e, H: e.dma_start(
                out=wstrv[jj][:, :, :],
                in_=sr[:, cc * 512:(cc + 1) * 512]
                .rearrange("(kc p) n -> p kc n", p=128))))(wsrc, c, c % 2),
                w=(wb,), sem="qp", n=16)
            if which == "q":
                for doi in range(4):
                    do = c * 4 + doi
                    b0 = (do % 2) * 2
                    for half in range(2):
                        parts = []
                        for kc in range(KC):
                            parts.append((bank(b0 + half),
                                          wstrv[c % 2][:, kc, doi * 128:(doi + 1) * 128],
                                          v_hfs[:, kc, half * 512:(half + 1) * 512],
                                          kc == 0, kc == KC - 1))
                        matmul_group((wb, b_hfs), (psb[b0 + half],), parts)
                        k.op("sc", (lambda dd, bb, hh: (lambda e, H: e.activation(
                            v_qfm[:, dd, hh * 512:(hh + 1) * 512], bank(bb),
                            AF.Relu, scale=INV_SCALE)))(do, b0 + half, half),
                            r=(psb[b0 + half],), w=(b_qfm,))
            else:
                for t in range(N_CS):
                    bi = 4 + (t % 2)
                    parts = []
                    for kc in range(KC):
                        parts.append((bank(bi),
                                      v_hfs[:, kc, t * 128:(t + 1) * 128],
                                      wstrv[c % 2][:, kc, :],
                                      kc == 0, kc == KC - 1))
                    matmul_group((wb, b_hfs), (psb[bi],), parts)
                    k.op("sc", (lambda tt, cc, bb: (lambda e, H: e.activation(
                        v_ut[:, tt, cc * 512:(cc + 1) * 512], bank(bb), AF.Silu)))(t, c, bi),
                        r=(psb[bi],), w=(b_ut,))

    for c in range(4):
        k.op("gp", (lambda cc: (lambda e, H: e.dma_start(
            out=v_wo[:, :, cc * 512:(cc + 1) * 512],
            in_=wo_d[:, cc * 512:(cc + 1) * 512]
            .rearrange("(kc p) n -> p kc n", p=128))))(c),
            w=(b_wo,), sem="qp", n=16)

    for t in range(N_CS):
        parts = []
        for h in range(NH):
            bi = h // 4
            col = (h % 4) * 128
            parts.append((ps[:, bi * 512 + col: bi * 512 + col + 128],
                          v_qfm[:, h, t * 128:(t + 1) * 128],
                          v_kvbf[:, h * 128:(h + 1) * 128],
                          True, True))
        matmul_group((b_qfm, b_kvbf), (psb[0], psb[1], psb[2], psb[3]), parts)
        k.op("sc", lambda e, H: e.activation(v_scr0, ps[:, 0:2048], AF.Square),
             r=(psb[0], psb[1], psb[2], psb[3]), w=(b_scr0,))
        k.op("ve", lambda e, H: e.tensor_reduce(
            v_ss, v_scr0.rearrange("p (h c) -> p h c", c=128), AX, ALU.add),
            r=(b_scr0,), w=(b_ss,))
        k.op("ve", lambda e, H: e.tensor_scalar_max(v_ss, v_ss, 1.28e-22),
             r=(b_ss,), w=(b_ss,))
        k.op("sc", lambda e, H: e.activation(v_rstd, v_ss, AF.Ln, scale=1.0 / DH),
             r=(b_ss,), w=(b_rstd,))
        k.op("sc", lambda e, H: e.activation(v_rstd, v_rstd, AF.Exp, scale=-0.5),
             r=(b_rstd,), w=(b_rstd,))
        for h in range(NH):
            k.op("ve", (lambda hh, tt: (lambda e, H: e.scalar_tensor_tensor(
                v_scr1b[:, hh * 128:(hh + 1) * 128],
                ps[:, (hh // 4) * 512 + (hh % 4) * 128: (hh // 4) * 512 + (hh % 4) * 128 + 128],
                v_rstd[0:128, hh:hh + 1],
                v_ut[:, tt, hh * 128:(hh + 1) * 128],
                ALU.mult, ALU.mult)))(h, t),
                r=(psb[h // 4], b_rstd, b_ut), w=(b_scr1b,))
        transpose_2048(v_scr1b, b_scr1b, lambda kc: v_suf[:, kc, :], b_suf,
                       banks=(4, 5))
        for c in range(4):
            bi = 6 + c % 2
            parts = []
            for kc in range(KC):
                parts.append((bank(bi), v_suf[:, kc, :],
                              v_wo[:, kc, c * 512:(c + 1) * 512],
                              kc == 0, kc == KC - 1))
            matmul_group((b_suf, b_wo), (psb[bi],), parts)
            k.op("sy", (lambda tt, cc: (lambda e, H: e.dma_start(
                out=v_xio0[:, cc * 512:(cc + 1) * 512],
                in_=xh[tt * 128:(tt + 1) * 128, cc * 512:(cc + 1) * 512])))(t, c),
                w=(b_xio0,), sem="qs", n=16)
            k.op("ve", (lambda cc, bb: (lambda e, H: e.tensor_add(
                v_xio1[:, cc * 512:(cc + 1) * 512], bank(bb),
                v_xio0[:, cc * 512:(cc + 1) * 512])))(c, bi),
                r=(psb[bi], b_xio0), w=(b_xio1,))
        k.op("sy", (lambda tt: (lambda e, H: e.dma_start(
            out=xres_d[tt * 128:(tt + 1) * 128, :], in_=v_xio1)))(t),
            r=(b_xio1,), w=(b_xres_d,), sem="qs", n=16)
        if debug:
            k.op("sy", (lambda tt: (lambda e, H: e.dma_start(
                out=dbg["xres"][tt * 128:(tt + 1) * 128, :], in_=v_xio1)))(t),
                r=(b_xio1,), sem="st", n=16)
        ln_tile(v_xio1, b_xio1, v_scr0b, b_scr0b)
        transpose_2048(v_scr0b, b_scr0b, lambda kc: v_h2c[:, kc, :], b_h2c,
                       banks=(4, 5))
        k.op("sy", (lambda tt: (lambda e, H: e.dma_start(
            out=h2_d[:, :, tt * 128:(tt + 1) * 128].rearrange("a p b -> p a b"),
            in_=v_h2c[:, :, :])))(t),
            r=(b_h2c,), w=(b_h2_d,), sem="qs", n=16)

    k.op("sy", lambda e, H: e.dma_start(
        out=v_h2f[:, :, :], in_=h2_d.rearrange("a p b -> p a b")),
        r=(b_h2_d,), w=(b_h2f,), sem="qs", n=16)

    for g in range(HC):
        r0, r1 = g * 512, (g + 1) * 512
        sgh, suh, sdh = stg_g, stg_u, stg_d
        extra = (b_staged,) if g == 0 else ()
        for kc in range(KC):
            k.op("sc", (lambda kcc, rr0, rr1, tt: (lambda e, H: e.dma_start(
                out=v_wgT[:, kcc, :], in_=tt[rr0:rr1, kcc * 128:(kcc + 1) * 128],
                transpose=True)))(kc, r0, r1, sgh),
                r=extra if kc == 0 else (), w=(b_wgT,), sem="qa", n=16)
            k.op("sc", (lambda kcc, rr0, rr1, tt: (lambda e, H: e.dma_start(
                out=v_wuT[:, kcc, :], in_=tt[rr0:rr1, kcc * 128:(kcc + 1) * 128],
                transpose=True)))(kc, r0, r1, suh),
                w=(b_wuT,), sem="qa", n=16)
        for hb in range(4):
            hcol = g * 512 + hb * 128
            k.op("sc", (lambda hbb, hc_, tt: (lambda e, H: e.dma_start(
                out=v_wdT[:, hbb, :], in_=tt[0:D, hc_:hc_ + 128],
                transpose=True)))(hb, hcol, sdh),
                r=extra if hb == 0 else (), w=(b_wdT,), sem="qa", n=16)
        for hb in range(4):
            rot = hb % 2
            bg0, bg1 = 2 * rot, 2 * rot + 1
            parts = []
            for kc in range(KC):
                parts.append((bank(bg0), v_wgT[:, kc, hb * 128:(hb + 1) * 128],
                              v_h2f[:, kc, 0:512], kc == 0, kc == KC - 1))
                parts.append((bank(bg1), v_wgT[:, kc, hb * 128:(hb + 1) * 128],
                              v_h2f[:, kc, 512:1024], kc == 0, kc == KC - 1))
            matmul_group((b_wgT, b_h2f), (psb[bg0], psb[bg1]), parts)
            for th, bb in ((0, bg0), (1, bg1)):
                sgv = v_sg if th == 0 else v_sg2
                sgb = b_sg if th == 0 else b_sg2
                k.op("sc", (lambda bbb, sv: (lambda e, H: e.activation(sv, bank(bbb), AF.Silu)))(bb, sgv),
                     r=(psb[bb],), w=(sgb,))
            bu0, bu1 = 4 + rot, 6 - rot * 0
            bu0, bu1 = (4, 5) if rot == 0 else (6, 7)
            parts = []
            for kc in range(KC):
                parts.append((bank(bu0), v_wuT[:, kc, hb * 128:(hb + 1) * 128],
                              v_h2f[:, kc, 0:512], kc == 0, kc == KC - 1))
                parts.append((bank(bu1), v_wuT[:, kc, hb * 128:(hb + 1) * 128],
                              v_h2f[:, kc, 512:1024], kc == 0, kc == KC - 1))
            matmul_group((b_wuT, b_h2f), (psb[bu0], psb[bu1]), parts)
            for th, bb in ((0, bu0), (1, bu1)):
                sgv = v_sg if th == 0 else v_sg2
                sgb = b_sg if th == 0 else b_sg2
                k.op("ve", (lambda hbb, thh, bbb, sv, sb_: (lambda e, H: e.tensor_mul(
                    v_gat[:, hbb, thh * 512:(thh + 1) * 512], sv, bank(bbb))))(hb, th, bb, sgv, sgb),
                    r=(sgb, psb[bb]), w=(b_gat,))
        if debug and g == 0:
            k.op("ve", lambda e, H: e.tensor_copy(
                v_scr0, v_wgT.rearrange("p a b -> p (a b)")[:, 0:2048]),
                r=(b_wgT,), w=(b_scr0,))
            k.op("sy", lambda e, H: e.dma_start(out=dbg["wg0"], in_=v_scr0),
                 r=(b_scr0,), sem="st", n=16)
            k.op("ve", lambda e, H: e.tensor_copy(
                v_scr0, v_gat.rearrange("p a b -> p (a b)")[:, 0:2048]),
                r=(b_gat,), w=(b_scr0,))
            k.op("sy", lambda e, H: e.dma_start(out=dbg["gat0"], in_=v_scr0),
                 r=(b_scr0,), sem="st", n=16)
            k.op("ve", lambda e, H: e.tensor_copy(
                v_scr0, v_h2f.rearrange("p a b -> p (a b)")[:, 0:2048]),
                r=(b_h2f,), w=(b_scr0,))
            k.op("sy", lambda e, H: e.dma_start(out=dbg["h2f0"], in_=v_scr0),
                 r=(b_scr0,), sem="st", n=16)
        for t in range(8):
            base = 4 if t % 2 == 0 else 0
            parts = []
            for hb in range(4):
                for dc in range(4):
                    parts.append((bank(base + dc),
                                  v_gat[:, hb, t * 128:(t + 1) * 128],
                                  v_wdT[:, hb, dc * 512:(dc + 1) * 512],
                                  hb == 0, hb == 3))
            matmul_group((b_gat, b_wdT),
                         (psb[base], psb[base + 1], psb[base + 2], psb[base + 3]),
                         parts)
            for dc in range(4):
                bi = base + dc
                if g == 0:
                    k.op("ve", (lambda tt, dcc, bb: (lambda e, H: e.tensor_copy(
                        v_acc[:, tt, dcc * 512:(dcc + 1) * 512], bank(bb))))(t, dc, bi),
                        r=(psb[bi],), w=(b_acc,))
                else:
                    k.op("ve", (lambda tt, dcc, bb: (lambda e, H: e.tensor_add(
                        v_acc[:, tt, dcc * 512:(dcc + 1) * 512],
                        v_acc[:, tt, dcc * 512:(dcc + 1) * 512],
                        bank(bb))))(t, dc, bi),
                        r=(psb[bi],), w=(b_acc,))

    for t in range(8):
        k.op("sy", (lambda tt: (lambda e, H: e.dma_start(
            out=v_xio0, in_=xres_d[tt * 128:(tt + 1) * 128, :])))(t),
            r=(b_xres_d,), w=(b_xio0,), sem="qs", n=16)
        k.op("ve", (lambda tt: (lambda e, H: e.tensor_add(
            v_xio1, v_acc[:, tt, :], v_xio0)))(t),
            r=(b_acc, b_xio0), w=(b_xio1,))
        k.op("sy", (lambda tt: (lambda e, H: e.dma_start(
            out=out_d[tt * 128:(tt + 1) * 128, :], in_=v_xio1)))(t),
            r=(b_xio1,), sem="st", n=16)

    import contextlib
    with contextlib.ExitStack() as stack:
        handles = {}
        for n_ in sem_names:
            handles[n_] = stack.enter_context(nc.semaphore(n_))
        block = stack.enter_context(nc.Block())

        def emit_for(ek, tail=None):
            def body(e):
                for waits, fn, sem, n_ in k.ops[ek]:
                    for (nm, val) in waits:
                        e.wait_ge(handles[nm], val)
                    instr = fn(e, handles)
                    if n_:
                        instr.then_inc(handles[sem], n_)
                if tail:
                    tail(e)
            return body

        block.tensor(emit_for("te"))
        block.scalar(emit_for("sc"))
        block.vector(emit_for("ve"))
        block.gpsimd(emit_for("gp"))
        block.sync(emit_for("sy", tail=lambda e: e.wait_ge(handles["st"],
                                                           k.cnt["st"])))

    global _LAST_SCHED
    _LAST_SCHED = k
    return nc


_NC_CACHE = {}


def _get_nc(debug=False):
    if debug not in _NC_CACHE:
        _NC_CACHE[debug] = _build_program(debug)
    return _NC_CACHE[debug]


import os as _os
_NONCE = [int.from_bytes(_os.urandom(4), "little") | 1]


def kernel(L0_attn_x, L0_attn_Wq, L0_attn_Wk, L0_attn_Wv, L0_attn_Wu,
           L0_attn_Wo, L0_ffn_w_gate, L0_ffn_w_up, L0_ffn_w_down,
           _debug=False):
    x = np.ascontiguousarray(np.asarray(L0_attn_x, dtype=np.float32)[0])
    wq = np.ascontiguousarray(np.asarray(L0_attn_Wq, dtype=np.float32))
    wk = np.ascontiguousarray(np.asarray(L0_attn_Wk, dtype=np.float32))
    wv = np.ascontiguousarray(np.asarray(L0_attn_Wv, dtype=np.float32))
    wu = np.ascontiguousarray(np.asarray(L0_attn_Wu, dtype=np.float32))
    wo = np.ascontiguousarray(np.asarray(L0_attn_Wo, dtype=np.float32))
    wg = np.ascontiguousarray(np.asarray(L0_ffn_w_gate, dtype=np.float32))
    wup = np.ascontiguousarray(np.asarray(L0_ffn_w_up, dtype=np.float32))
    wd = np.ascontiguousarray(np.asarray(L0_ffn_w_down, dtype=np.float32))

    nc = _get_nc(_debug)
    in_maps = []
    slices = []
    for r in range(NCORE):
        p, half = r // 2, r % 2
        s0 = half * HLF + p * S
        slices.append(s0)
        rest = np.concatenate([x[:s0], x[s0 + S:]], axis=0)
        xh_r = np.ascontiguousarray(np.concatenate([x[s0:s0 + S], rest], axis=0))
        in_maps.append({
            "xh": xh_r,
            "wq": wq, "wk": wk, "wv": wv, "wu": wu, "wo": wo,
            "wgf": wg, "wuf": wup, "wdf": wd,
        })
    res = run_bass_kernel_spmd(nc, in_maps, core_ids=list(range(NCORE)))
    out = np.empty((L, D), dtype=np.float32)
    for r in range(NCORE):
        out[slices[r]: slices[r] + S] = res.results[r]["out"]
    if _debug:
        return out.reshape(1, L, D), res
    return out.reshape(1, L, D)


# version 06
# speedup vs baseline: 1.0473x; 1.0473x over previous
import math
import numpy as np

import concourse.bass as bass
import concourse.mybir as mybir
from concourse.bass_utils import run_bass_kernel_spmd

F32 = mybir.dt.float32
BF16 = mybir.dt.bfloat16
U32 = mybir.dt.uint32
AF = mybir.ActivationFunctionType
ALU = mybir.AluOpType
AX = mybir.AxisListType.X

D = 2048
HID = 8192
L = 8192
HLF = 4096
S = 1024
NH = 16
DH = 128
NCORE = 8
INV_SCALE = 1.0 / math.sqrt(DH)
N_CH = HLF // 128
N_CS = S // 128
KC = D // 128
HC = HID // 512
EPS = 1e-5

CHAIN = {"te": "pe", "sc": "act", "ve": "dve", "gp": "plc", "sy": "syc"}


class Buf:
    __slots__ = ("name", "lo", "hi", "ready", "readers")

    def __init__(self, name, lo=None, hi=None):
        self.name = name
        self.lo = lo
        self.hi = hi
        self.ready = None
        self.readers = []


class Sched:
    def __init__(self, sem_names):
        self.cnt = {n: 0 for n in sem_names}
        self.ops = {e: [] for e in CHAIN}
        self.wm = {e: {n: 0 for n in sem_names} for e in CHAIN}
        self.sbufs = []

    def reg(self, buf):
        self.sbufs.append(buf)
        return buf

    def inc(self, name, n=1):
        self.cnt[name] += n
        return (name, self.cnt[name])

    def _waits(self, ek, toks):
        out = []
        for t in toks:
            if t is None:
                continue
            name, val = t
            assert val <= self.cnt[name], (name, val, self.cnt[name])
            if self.wm[ek][name] >= val:
                continue
            self.wm[ek][name] = val
            out.append((name, val))
        return out

    def op(self, ek, fn, r=(), w=(), sem=None, n=1):
        toks = []
        for b in r:
            toks.append(b.ready)
        for b in w:
            toks.append(b.ready)
            toks.extend(b.readers)
            if b.lo is not None:
                for o in self.sbufs:
                    if o is b or o.lo is None:
                        continue
                    if o.lo < b.hi and b.lo < o.hi:
                        toks.append(o.ready)
                        toks.extend(o.readers)
                        if b.lo <= o.lo and b.hi >= o.hi:
                            o.ready = None
                            o.readers = []
                        else:
                            pass
        waits = self._waits(ek, toks)
        if sem is None:
            sem = CHAIN[ek]
        tok = self.inc(sem, n)
        self.ops[ek].append((waits, fn, sem, n))
        for b in r:
            b.readers.append(tok)
        for b in w:
            b.ready = tok
            b.readers = []
        return tok

    def raw(self, ek, fn, waits=(), incs=()):
        w = self._waits(ek, [t for t in waits])
        for (s_, n_) in incs:
            self.inc(s_, n_)
        self.ops[ek].append((w, fn, None, 0))


def _build_program(debug=False):
    nc = bass.Bass("TRN2", target_bir_lowering=False, num_devices=NCORE)

    xh = nc.dram_tensor("xh", [L, D], F32, kind="ExternalInput").ap()
    wq_d = nc.dram_tensor("wq", [D, D], F32, kind="ExternalInput").ap()
    wk_d = nc.dram_tensor("wk", [D, D], F32, kind="ExternalInput").ap()
    wv_d = nc.dram_tensor("wv", [D, D], F32, kind="ExternalInput").ap()
    wu_d = nc.dram_tensor("wu", [D, D], F32, kind="ExternalInput").ap()
    wo_d = nc.dram_tensor("wo", [D, D], F32, kind="ExternalInput").ap()
    wgf = nc.dram_tensor("wgf", [HID, D], F32, kind="ExternalInput").ap()
    wuf = nc.dram_tensor("wuf", [HID, D], F32, kind="ExternalInput").ap()
    wdf = nc.dram_tensor("wdf", [D, HID], F32, kind="ExternalInput").ap()
    out_d = nc.dram_tensor("out", [S, D], F32, kind="ExternalOutput").ap()
    dbg = {}
    if debug:
        dbg["kv"] = nc.dram_tensor("dbg_kv", [128, D], F32, kind="ExternalOutput").ap()
        dbg["xres"] = nc.dram_tensor("dbg_xres", [S, D], F32, kind="ExternalOutput").ap()
        dbg["h0"] = nc.dram_tensor("dbg_h0", [128, D], F32, kind="ExternalOutput").ap()
        dbg["wg0"] = nc.dram_tensor("dbg_wg0", [128, D], F32, kind="ExternalOutput").ap()
        dbg["gat0"] = nc.dram_tensor("dbg_gat0", [128, D], F32, kind="ExternalOutput").ap()
        dbg["h2f0"] = nc.dram_tensor("dbg_h2f0", [128, D], F32, kind="ExternalOutput").ap()
        dbg["kl"] = nc.dram_tensor("dbg_kl", [128, D], F32, kind="ExternalOutput").ap()
        dbg["vl"] = nc.dram_tensor("dbg_vl", [128, D], F32, kind="ExternalOutput").ap()


    stg_g = nc.dram_tensor("stg_g", [HID, D], BF16)
    stg_u = nc.dram_tensor("stg_u", [HID, D], BF16)
    stg_d = nc.dram_tensor("stg_d", [D, HID], BF16)
    xres_d = nc.dram_tensor("xres_d", [S, D], F32)
    h2_d = nc.dram_tensor("h2_d", [KC, 128, S], BF16)
    hfs_d = nc.dram_tensor("hfs_d", [KC, 128, S], BF16)

    pid = nc.partition_id_tensor

    ARENA = 204800
    arena = nc.alloc_sbuf_tensor("arena", [128, ARENA], mybir.dt.uint8).ap()

    sem_names = ["qs", "qp", "qa", "pe", "act", "dve", "plc", "syc", "po", "st"]
    k = Sched(sem_names)

    def view(name, off, dtype, *shape):
        es = {F32: 4, BF16: 2, U32: 4}[dtype]
        tot = 1
        for s_ in shape:
            tot *= s_
        assert off % 64 == 0 and off + tot * es <= ARENA, (name, off, shape)
        ap = arena.bitcast(dtype)[:, off // es: off // es + tot]
        if len(shape) == 2:
            ap = ap.rearrange("p (a b) -> p a b", b=shape[1])
        elif len(shape) == 3:
            ap = ap.rearrange("p (a b c) -> p a b c", b=shape[1], c=shape[2])
        buf = k.reg(Buf(name, off, off + tot * es))
        return ap, buf

    IDN, b_idn = view("idn", 0, BF16, 128)
    v_pid, b_pid = view("pid", 512, U32, 1)
    v_non, b_non = view("non", 576, U32, 8)
    v_fl, b_fl = view("fl", 640, U32, 2)
    v_msum, b_msum = view("msum", 704, F32, 1)
    v_m, b_m = view("m", 768, F32, 1)
    v_msq, b_msq = view("msq", 832, F32, 1)
    v_var, b_var = view("var", 896, F32, 1)
    v_rstd1, b_rstd1 = view("rstd1", 960, F32, 1)
    v_ssq, b_ssq = view("ssq", 1024, F32, 1)
    v_ss, b_ss = view("ss", 1088, F32, 16)
    v_rstd, b_rstd = view("rstd", 1152, F32, 16)
    v_ones, b_ones = view("ones", 1280, F32, 128)
    SCR0, SCR1 = 2048, 2048 + 8192
    v_scr0, b_scr0 = view("scr0", SCR0, F32, 2048)
    v_scr1, b_scr1 = view("scr1", SCR1, F32, 2048)
    v_scr0b, b_scr0b = view("scr0b", SCR0, BF16, 2048)
    v_scr1b, b_scr1b = view("scr1b", SCR1, BF16, 2048)
    XIO0 = SCR1 + 8192
    XIO1 = XIO0 + 8192
    v_xio0, b_xio0 = view("xio0", XIO0, F32, 2048)
    v_xio1, b_xio1 = view("xio1", XIO1, F32, 2048)
    KVA = XIO1 + 8192
    v_kvsb, b_kvsb = view("kvsb", KVA, F32, 2048)
    v_kvot, b_kvot = view("kvot", KVA + 8192, F32, 2048)
    v_kvbf, b_kvbf = view("kvbf", KVA + 16384, BF16, 2048)
    A1B = KVA + 20480
    v_wk, b_wk = view("wk", A1B, BF16, 16, 2048)
    v_wv, b_wv = view("wv", A1B + 65536, BF16, 16, 2048)
    HCH0 = A1B + 131072
    v_hchA, b_hchA = view("hchA", HCH0, BF16, 16, 128)
    v_hchB, b_hchB = view("hchB", HCH0 + 4096, BF16, 16, 128)
    v_kt, b_kt = view("kt", HCH0 + 8192, BF16, 2048)
    v_vt, b_vt = view("vt", HCH0 + 12288, BF16, 2048)
    assert HCH0 + 16384 <= ARENA
    v_qfm, b_qfm = view("qfm", A1B, BF16, 16, 1024)
    v_ut, b_ut = view("ut", A1B + 32768, BF16, 8, 2048)
    WOB = A1B + 65536
    v_hfs, b_hfs = view("hfs", WOB, BF16, 16, 1024)
    v_wo, b_wo = view("wo", WOB, BF16, 16, 2048)
    WSTR = WOB + 32768
    v_wstr0, b_wstr0 = view("wstr0", WSTR, BF16, 16, 512)
    v_wstr1, b_wstr1 = view("wstr1", WSTR + 16384, BF16, 16, 512)
    SUF = WOB + 65536
    v_suf, b_suf = view("suf", SUF, BF16, 16, 128)
    v_h2c, b_h2c = view("h2c", SUF + 4096, BF16, 16, 128)
    v_acc, b_acc = view("acc", A1B, F32, 8, 2048)
    v_h2f, b_h2f = view("h2f", A1B + 65536, BF16, 16, 1024)
    FW = A1B + 98304
    v_wgT, b_wgT = view("wgT", FW, BF16, 16, 512)
    v_wuT, b_wuT = view("wuT", FW + 16384, BF16, 16, 512)
    v_wdT, b_wdT = view("wdT", FW + 32768, BF16, 4, 2048)
    v_gat, b_gat = view("gat", KVA, BF16, 4, 1024)
    v_sg, b_sg = view("sg", KVA + 8192, BF16, 512)
    v_sg2, b_sg2 = view("sg2", KVA + 9216, BF16, 512)
    assert FW + 49152 <= ARENA

    ps = nc.alloc_psum_tensor("ps", [128, 4096], F32).ap()

    def bank(i):
        return ps[:, i * 512:(i + 1) * 512]

    psb = [Buf(f"psb{i}") for i in range(8)]

    b_staged = Buf("staged")
    b_r0 = Buf("r0")
    b_xres_d = Buf("xres_d")
    b_h2_d = Buf("h2_d")
    b_hfs_d = Buf("hfs_d")
    x_scrbufs = [b_scr0, b_scr1]

    def matmul_group(reads, banks, parts):
        nparts = len(parts)
        for i, (o, l_, r_, st_, sp_) in enumerate(parts):
            fn = (lambda oo, ll, rr, ss, pp: (lambda e, H: e.matmul(
                oo, ll, rr, start=ss, stop=pp, skip_group_check=True)))(o, l_, r_, st_, sp_)
            if i == 0 and i == nparts - 1:
                tok = k.op("te", fn, r=reads, w=banks)
            elif i == 0:
                toks = [b.ready for b in reads]
                for b in banks:
                    toks.append(b.ready)
                    toks.extend(b.readers)
                waits = k._waits("te", toks)
                k.ops["te"].append((waits, fn, "pe", 0))
            elif i == nparts - 1:
                tok = k.inc("pe", 1)
                k.ops["te"].append(((), fn, "pe", 1))
            else:
                k.ops["te"].append(((), fn, "pe", 0))
        for b in reads:
            b.readers.append(tok)
        for b in banks:
            b.ready = tok
            b.readers = []
        return tok

    def ln_tile(xv, xbuf, scrb_ap, scrb_buf):
        k.op("sc", (lambda sb, xx: (lambda e, H: e.activation(
            sb, xx, AF.Square, accum_out=v_ssq[0:128, 0:1])))(scrb_ap, xv),
            r=(xbuf,), w=(scrb_buf, b_ssq))
        k.op("ve", (lambda xx: (lambda e, H: e.reduce_sum(v_msum, xx, AX)))(xv),
             r=(xbuf,), w=(b_msum,))
        k.op("ve", lambda e, H: e.tensor_scalar_mul(v_m, v_msum, 1.0 / D),
             r=(b_msum,), w=(b_m,))
        k.op("ve", lambda e, H: e.tensor_mul(v_msq, v_m, v_m),
             r=(b_m,), w=(b_msq,))
        k.op("ve", lambda e, H: e.tensor_scalar(v_var, v_ssq, 1.0 / D,
                                                v_msq[0:128, 0:1],
                                                ALU.mult, ALU.subtract),
             r=(b_ssq, b_msq), w=(b_var,))
        k.op("ve", lambda e, H: e.tensor_scalar_add(v_var, v_var, EPS),
             r=(b_var,), w=(b_var,))
        k.op("sc", lambda e, H: e.activation(v_rstd1, v_var, AF.Ln),
             r=(b_var,), w=(b_rstd1,))
        k.op("sc", lambda e, H: e.activation(v_rstd1, v_rstd1, AF.Exp, scale=-0.5),
             r=(b_rstd1,), w=(b_rstd1,))
        k.op("ve", (lambda sb, xx: (lambda e, H: e.tensor_scalar(
            sb, xx, v_m[0:128, 0:1], v_rstd1[0:128, 0:1],
            ALU.subtract, ALU.mult)))(scrb_ap, xv),
            r=(xbuf, b_m, b_rstd1), w=(scrb_buf,))

    def transpose_2048(src_bf, src_buf, dst_ap_fn, dst_buf, banks=(2, 3)):
        for grp in range(4):
            bi = banks[grp % 2]
            parts = []
            for j in range(4):
                kc = grp * 4 + j
                parts.append((bank(bi)[:, j * 128:(j + 1) * 128],
                              src_bf[:, kc * 128:(kc + 1) * 128], IDN,
                              True, True))
            matmul_group((src_buf, b_idn), (psb[bi],), parts)
            for j in range(4):
                kc = grp * 4 + j
                k.op("ve", (lambda o, i_: (lambda e, H: e.tensor_copy(o, i_)))(
                    dst_ap_fn(kc), bank(bi)[:, j * 128:(j + 1) * 128]),
                    r=(psb[bi],), w=(dst_buf,))

    k.op("gp", lambda e, H: e.memset(v_ones, 1.0), w=(b_ones,))
    k.op("gp", lambda e, H: e.affine_select(IDN, v_ones, [[1, 128]],
                                            ALU.is_equal, 0.0, base=0,
                                            channel_multiplier=-1),
         r=(b_ones,), w=(b_idn,))
    for src, dstv, dbuf in ((wk_d, v_wk, b_wk), (wv_d, v_wv, b_wv)):
        for c in range(4):
            k.op("gp", (lambda dv, sr, cc: (lambda e, H: e.dma_start(
                out=dv[:, :, cc * 512:(cc + 1) * 512],
                in_=sr[:, cc * 512:(cc + 1) * 512]
                .rearrange("(kc p) n -> p kc n", p=128))))(dstv, src, c),
                w=(dbuf,), sem="qp", n=16)

    for tsrc, tdst, nchunk, rows in ((wgf, stg_g, 16, 512), (wuf, stg_u, 16, 512),
                                     (wdf, stg_d, 8, 256)):
        for c in range(nchunk):
            r0, r1 = c * rows, (c + 1) * rows
            k.op("gp", (lambda a, b2, rr0, rr1: (lambda e, H: e.dma_start(
                out=b2[rr0:rr1, :], in_=a[rr0:rr1, :])))(tsrc, tdst, r0, r1),
                w=(b_staged,), sem="qp", n=16)

    NT_ALL = L // 128

    def emit_ln(t):
        j = t % 2
        xb = b_xio0 if j == 0 else b_xio1
        xv = v_xio0 if j == 0 else v_xio1
        k.op("sy", (lambda vv, tt: (lambda e, H: e.dma_start(
            out=vv, in_=xh[tt * 128:(tt + 1) * 128, :])))(xv, t),
            w=(xb,), sem="qs", n=16)
        scrb_ap = v_scr0b if j == 0 else v_scr1b
        scrb_buf = b_scr0b if j == 0 else b_scr1b
        ln_tile(xv, xb, scrb_ap, scrb_buf)

    emit_ln(0)
    for t in range(NT_ALL):
        j = t % 2
        if t + 1 < NT_ALL:
            emit_ln(t + 1)
        scrb_ap = v_scr0b if j == 0 else v_scr1b
        scrb_buf = b_scr0b if j == 0 else b_scr1b
        hch_ap = v_hchA if j == 0 else v_hchB
        hch_buf = b_hchA if j == 0 else b_hchB
        transpose_2048(scrb_ap, scrb_buf,
                       (lambda ha: (lambda kc: ha[:, kc, :]))(hch_ap), hch_buf,
                       banks=(4, 5))
        if t < N_CS:
            k.op("sy", (lambda ha, tt: (lambda e, H: e.dma_start(
                out=hfs_d[:, :, tt * 128:(tt + 1) * 128].rearrange("a p b -> p a b"),
                in_=ha[:, :, :])))(hch_ap, t),
                r=(hch_buf,), w=(b_hfs_d,), sem="qs", n=16)
        for wi, (wsb, wbuf, dstv, dbuf, use_act, scale) in enumerate((
                (v_wk, b_wk, v_kt, b_kt, True, INV_SCALE),
                (v_wv, b_wv, v_vt, b_vt, False, 1.0))):
            for c in range(4):
                bi = (wi * 4 + c) % 4
                parts = []
                for kc in range(KC):
                    parts.append((bank(bi), hch_ap[:, kc, :],
                                  wsb[:, kc, c * 512:(c + 1) * 512],
                                  kc == 0, kc == KC - 1))
                matmul_group((hch_buf, wbuf), (psb[bi],), parts)
                if use_act:
                    k.op("sc", (lambda dv, cc, bb, sc_: (lambda e, H: e.activation(
                        dv[:, cc * 512:(cc + 1) * 512], bank(bb), AF.Relu, scale=sc_)))(
                        dstv, c, bi, scale),
                        r=(psb[bi],), w=(dbuf,))
                else:
                    k.op("ve", (lambda dv, cc, bb: (lambda e, H: e.tensor_copy(
                        dv[:, cc * 512:(cc + 1) * 512], bank(bb))))(dstv, c, bi),
                        r=(psb[bi],), w=(dbuf,))
        if debug and t == 0:
            k.op("ve", (lambda ha: (lambda e, H: e.tensor_copy(
                v_scr0, ha.rearrange("p a b -> p (a b)"))))(hch_ap),
                r=(hch_buf,), w=(b_scr0,))
            k.op("sy", lambda e, H: e.dma_start(out=dbg["h0"], in_=v_scr0),
                 r=(b_scr0,), sem="st", n=16)
        if debug and t == NT_ALL - 1:
            k.op("ve", lambda e, H: e.tensor_copy(v_scr0, v_kt),
                 r=(b_kt,), w=(b_scr0,))
            k.op("sy", lambda e, H: e.dma_start(out=dbg["kl"], in_=v_scr0),
                 r=(b_scr0,), sem="st", n=16)
            k.op("ve", lambda e, H: e.tensor_copy(v_scr1, v_vt),
                 r=(b_vt,), w=(b_scr1,))
            k.op("sy", lambda e, H: e.dma_start(out=dbg["vl"], in_=v_scr1),
                 r=(b_scr1,), sem="st", n=16)
        parts = []
        for h in range(NH):
            bi = 4 + h // 4
            col = (h % 4) * 128
            parts.append((ps[:, bi * 512 + col: bi * 512 + col + 128],
                          v_kt[:, h * 128:(h + 1) * 128],
                          v_vt[:, h * 128:(h + 1) * 128],
                          True, True))
        matmul_group((b_kt, b_vt), (psb[4], psb[5], psb[6], psb[7]), parts)
        for c in range(4):
            if t == 0:
                k.op("ve", (lambda cc: (lambda e, H: e.tensor_copy(
                    v_kvsb[:, cc * 512:(cc + 1) * 512], bank(4 + cc))))(c),
                    r=(psb[4 + c],), w=(b_kvsb,))
            else:
                k.op("ve", (lambda cc: (lambda e, H: e.tensor_add(
                    v_kvsb[:, cc * 512:(cc + 1) * 512],
                    v_kvsb[:, cc * 512:(cc + 1) * 512],
                    bank(4 + cc))))(c),
                    r=(psb[4 + c],), w=(b_kvsb,))

    k.op("ve", lambda e, H: e.tensor_copy(v_kvbf, v_kvsb),
         r=(b_kvsb,), w=(b_kvbf,))
    if debug:
        k.op("sy", lambda e, H: e.dma_start(out=dbg["kv"], in_=v_kvsb),
             r=(b_kvsb,), sem="st", n=16)

    k.op("sy", lambda e, H: e.dma_start(
        out=v_hfs[:, :, :], in_=hfs_d.rearrange("a p b -> p a b")),
        r=(b_hfs_d,), w=(b_hfs,), sem="qs", n=16)

    wstrv = [v_wstr0, v_wstr1]
    wstrb = [b_wstr0, b_wstr1]
    for wsrc, which in ((wq_d, "q"), (wu_d, "u")):
        for c in range(4):
            wb = wstrb[c % 2]
            k.op("gp", (lambda sr, cc, jj: (lambda e, H: e.dma_start(
                out=wstrv[jj][:, :, :],
                in_=sr[:, cc * 512:(cc + 1) * 512]
                .rearrange("(kc p) n -> p kc n", p=128))))(wsrc, c, c % 2),
                w=(wb,), sem="qp", n=16)
            if which == "q":
                for doi in range(4):
                    do = c * 4 + doi
                    b0 = (do % 2) * 2
                    for half in range(2):
                        parts = []
                        for kc in range(KC):
                            parts.append((bank(b0 + half),
                                          wstrv[c % 2][:, kc, doi * 128:(doi + 1) * 128],
                                          v_hfs[:, kc, half * 512:(half + 1) * 512],
                                          kc == 0, kc == KC - 1))
                        matmul_group((wb, b_hfs), (psb[b0 + half],), parts)
                        k.op("sc", (lambda dd, bb, hh: (lambda e, H: e.activation(
                            v_qfm[:, dd, hh * 512:(hh + 1) * 512], bank(bb),
                            AF.Relu, scale=INV_SCALE)))(do, b0 + half, half),
                            r=(psb[b0 + half],), w=(b_qfm,))
            else:
                for t in range(N_CS):
                    bi = 4 + (t % 2)
                    parts = []
                    for kc in range(KC):
                        parts.append((bank(bi),
                                      v_hfs[:, kc, t * 128:(t + 1) * 128],
                                      wstrv[c % 2][:, kc, :],
                                      kc == 0, kc == KC - 1))
                    matmul_group((wb, b_hfs), (psb[bi],), parts)
                    k.op("sc", (lambda tt, cc, bb: (lambda e, H: e.activation(
                        v_ut[:, tt, cc * 512:(cc + 1) * 512], bank(bb), AF.Silu)))(t, c, bi),
                        r=(psb[bi],), w=(b_ut,))

    for c in range(4):
        k.op("gp", (lambda cc: (lambda e, H: e.dma_start(
            out=v_wo[:, :, cc * 512:(cc + 1) * 512],
            in_=wo_d[:, cc * 512:(cc + 1) * 512]
            .rearrange("(kc p) n -> p kc n", p=128))))(c),
            w=(b_wo,), sem="qp", n=16)

    for t in range(N_CS):
        parts = []
        for h in range(NH):
            bi = h // 4
            col = (h % 4) * 128
            parts.append((ps[:, bi * 512 + col: bi * 512 + col + 128],
                          v_qfm[:, h, t * 128:(t + 1) * 128],
                          v_kvbf[:, h * 128:(h + 1) * 128],
                          True, True))
        matmul_group((b_qfm, b_kvbf), (psb[0], psb[1], psb[2], psb[3]), parts)
        k.op("sc", lambda e, H: e.activation(v_scr0, ps[:, 0:2048], AF.Square),
             r=(psb[0], psb[1], psb[2], psb[3]), w=(b_scr0,))
        k.op("ve", lambda e, H: e.tensor_reduce(
            v_ss, v_scr0.rearrange("p (h c) -> p h c", c=128), AX, ALU.add),
            r=(b_scr0,), w=(b_ss,))
        k.op("ve", lambda e, H: e.tensor_scalar_max(v_ss, v_ss, 1.28e-22),
             r=(b_ss,), w=(b_ss,))
        k.op("sc", lambda e, H: e.activation(v_rstd, v_ss, AF.Ln, scale=1.0 / DH),
             r=(b_ss,), w=(b_rstd,))
        k.op("sc", lambda e, H: e.activation(v_rstd, v_rstd, AF.Exp, scale=-0.5),
             r=(b_rstd,), w=(b_rstd,))
        for h in range(NH):
            k.op("ve", (lambda hh, tt: (lambda e, H: e.scalar_tensor_tensor(
                v_scr1b[:, hh * 128:(hh + 1) * 128],
                ps[:, (hh // 4) * 512 + (hh % 4) * 128: (hh // 4) * 512 + (hh % 4) * 128 + 128],
                v_rstd[0:128, hh:hh + 1],
                v_ut[:, tt, hh * 128:(hh + 1) * 128],
                ALU.mult, ALU.mult)))(h, t),
                r=(psb[h // 4], b_rstd, b_ut), w=(b_scr1b,))
        transpose_2048(v_scr1b, b_scr1b, lambda kc: v_suf[:, kc, :], b_suf,
                       banks=(4, 5))
        for c in range(4):
            bi = 6 + c % 2
            parts = []
            for kc in range(KC):
                parts.append((bank(bi), v_suf[:, kc, :],
                              v_wo[:, kc, c * 512:(c + 1) * 512],
                              kc == 0, kc == KC - 1))
            matmul_group((b_suf, b_wo), (psb[bi],), parts)
            k.op("sy", (lambda tt, cc: (lambda e, H: e.dma_start(
                out=v_xio0[:, cc * 512:(cc + 1) * 512],
                in_=xh[tt * 128:(tt + 1) * 128, cc * 512:(cc + 1) * 512])))(t, c),
                w=(b_xio0,), sem="qs", n=16)
            k.op("ve", (lambda cc, bb: (lambda e, H: e.tensor_add(
                v_xio1[:, cc * 512:(cc + 1) * 512], bank(bb),
                v_xio0[:, cc * 512:(cc + 1) * 512])))(c, bi),
                r=(psb[bi], b_xio0), w=(b_xio1,))
        k.op("sy", (lambda tt: (lambda e, H: e.dma_start(
            out=xres_d[tt * 128:(tt + 1) * 128, :], in_=v_xio1)))(t),
            r=(b_xio1,), w=(b_xres_d,), sem="qs", n=16)
        if debug:
            k.op("sy", (lambda tt: (lambda e, H: e.dma_start(
                out=dbg["xres"][tt * 128:(tt + 1) * 128, :], in_=v_xio1)))(t),
                r=(b_xio1,), sem="st", n=16)
        ln_tile(v_xio1, b_xio1, v_scr0b, b_scr0b)
        transpose_2048(v_scr0b, b_scr0b, lambda kc: v_h2c[:, kc, :], b_h2c,
                       banks=(4, 5))
        k.op("sy", (lambda tt: (lambda e, H: e.dma_start(
            out=h2_d[:, :, tt * 128:(tt + 1) * 128].rearrange("a p b -> p a b"),
            in_=v_h2c[:, :, :])))(t),
            r=(b_h2c,), w=(b_h2_d,), sem="qs", n=16)

    k.op("sy", lambda e, H: e.dma_start(
        out=v_h2f[:, :, :], in_=h2_d.rearrange("a p b -> p a b")),
        r=(b_h2_d,), w=(b_h2f,), sem="qs", n=16)

    for g in range(HC):
        r0, r1 = g * 512, (g + 1) * 512
        sgh, suh, sdh = stg_g, stg_u, stg_d
        extra = (b_staged,) if g == 0 else ()
        for kc in range(KC):
            k.op("sc", (lambda kcc, rr0, rr1, tt: (lambda e, H: e.dma_start(
                out=v_wgT[:, kcc, :], in_=tt[rr0:rr1, kcc * 128:(kcc + 1) * 128],
                transpose=True)))(kc, r0, r1, sgh),
                r=extra if kc == 0 else (), w=(b_wgT,), sem="qa", n=16)
            k.op("sc", (lambda kcc, rr0, rr1, tt: (lambda e, H: e.dma_start(
                out=v_wuT[:, kcc, :], in_=tt[rr0:rr1, kcc * 128:(kcc + 1) * 128],
                transpose=True)))(kc, r0, r1, suh),
                w=(b_wuT,), sem="qa", n=16)
        for hb in range(4):
            hcol = g * 512 + hb * 128
            k.op("sc", (lambda hbb, hc_, tt: (lambda e, H: e.dma_start(
                out=v_wdT[:, hbb, :], in_=tt[0:D, hc_:hc_ + 128],
                transpose=True)))(hb, hcol, sdh),
                r=extra if hb == 0 else (), w=(b_wdT,), sem="qa", n=16)
        for hb in range(4):
            rot = hb % 2
            bg0, bg1 = 2 * rot, 2 * rot + 1
            parts = []
            for kc in range(KC):
                parts.append((bank(bg0), v_wgT[:, kc, hb * 128:(hb + 1) * 128],
                              v_h2f[:, kc, 0:512], kc == 0, kc == KC - 1))
                parts.append((bank(bg1), v_wgT[:, kc, hb * 128:(hb + 1) * 128],
                              v_h2f[:, kc, 512:1024], kc == 0, kc == KC - 1))
            matmul_group((b_wgT, b_h2f), (psb[bg0], psb[bg1]), parts)
            for th, bb in ((0, bg0), (1, bg1)):
                sgv = v_sg if th == 0 else v_sg2
                sgb = b_sg if th == 0 else b_sg2
                k.op("sc", (lambda bbb, sv: (lambda e, H: e.activation(sv, bank(bbb), AF.Silu)))(bb, sgv),
                     r=(psb[bb],), w=(sgb,))
            bu0, bu1 = 4 + rot, 6 - rot * 0
            bu0, bu1 = (4, 5) if rot == 0 else (6, 7)
            parts = []
            for kc in range(KC):
                parts.append((bank(bu0), v_wuT[:, kc, hb * 128:(hb + 1) * 128],
                              v_h2f[:, kc, 0:512], kc == 0, kc == KC - 1))
                parts.append((bank(bu1), v_wuT[:, kc, hb * 128:(hb + 1) * 128],
                              v_h2f[:, kc, 512:1024], kc == 0, kc == KC - 1))
            matmul_group((b_wuT, b_h2f), (psb[bu0], psb[bu1]), parts)
            for th, bb in ((0, bu0), (1, bu1)):
                sgv = v_sg if th == 0 else v_sg2
                sgb = b_sg if th == 0 else b_sg2
                k.op("ve", (lambda hbb, thh, bbb, sv, sb_: (lambda e, H: e.tensor_mul(
                    v_gat[:, hbb, thh * 512:(thh + 1) * 512], sv, bank(bbb))))(hb, th, bb, sgv, sgb),
                    r=(sgb, psb[bb]), w=(b_gat,))
        if debug and g == 0:
            k.op("ve", lambda e, H: e.tensor_copy(
                v_scr0, v_wgT.rearrange("p a b -> p (a b)")[:, 0:2048]),
                r=(b_wgT,), w=(b_scr0,))
            k.op("sy", lambda e, H: e.dma_start(out=dbg["wg0"], in_=v_scr0),
                 r=(b_scr0,), sem="st", n=16)
            k.op("ve", lambda e, H: e.tensor_copy(
                v_scr0, v_gat.rearrange("p a b -> p (a b)")[:, 0:2048]),
                r=(b_gat,), w=(b_scr0,))
            k.op("sy", lambda e, H: e.dma_start(out=dbg["gat0"], in_=v_scr0),
                 r=(b_scr0,), sem="st", n=16)
            k.op("ve", lambda e, H: e.tensor_copy(
                v_scr0, v_h2f.rearrange("p a b -> p (a b)")[:, 0:2048]),
                r=(b_h2f,), w=(b_scr0,))
            k.op("sy", lambda e, H: e.dma_start(out=dbg["h2f0"], in_=v_scr0),
                 r=(b_scr0,), sem="st", n=16)
        for t in range(8):
            base = 4 if t % 2 == 0 else 0
            parts = []
            for hb in range(4):
                for dc in range(4):
                    parts.append((bank(base + dc),
                                  v_gat[:, hb, t * 128:(t + 1) * 128],
                                  v_wdT[:, hb, dc * 512:(dc + 1) * 512],
                                  hb == 0, hb == 3))
            matmul_group((b_gat, b_wdT),
                         (psb[base], psb[base + 1], psb[base + 2], psb[base + 3]),
                         parts)
            for dc in range(4):
                bi = base + dc
                if g == 0:
                    k.op("ve", (lambda tt, dcc, bb: (lambda e, H: e.tensor_copy(
                        v_acc[:, tt, dcc * 512:(dcc + 1) * 512], bank(bb))))(t, dc, bi),
                        r=(psb[bi],), w=(b_acc,))
                else:
                    k.op("ve", (lambda tt, dcc, bb: (lambda e, H: e.tensor_add(
                        v_acc[:, tt, dcc * 512:(dcc + 1) * 512],
                        v_acc[:, tt, dcc * 512:(dcc + 1) * 512],
                        bank(bb))))(t, dc, bi),
                        r=(psb[bi],), w=(b_acc,))

    for t in range(8):
        k.op("sy", (lambda tt: (lambda e, H: e.dma_start(
            out=v_xio0, in_=xres_d[tt * 128:(tt + 1) * 128, :])))(t),
            r=(b_xres_d,), w=(b_xio0,), sem="qs", n=16)
        k.op("ve", (lambda tt: (lambda e, H: e.tensor_add(
            v_xio1, v_acc[:, tt, :], v_xio0)))(t),
            r=(b_acc, b_xio0), w=(b_xio1,))
        k.op("sy", (lambda tt: (lambda e, H: e.dma_start(
            out=out_d[tt * 128:(tt + 1) * 128, :], in_=v_xio1)))(t),
            r=(b_xio1,), sem="st", n=16)

    import contextlib
    with contextlib.ExitStack() as stack:
        handles = {}
        for n_ in sem_names:
            handles[n_] = stack.enter_context(nc.semaphore(n_))
        block = stack.enter_context(nc.Block())

        def emit_for(ek, tail=None):
            def body(e):
                for waits, fn, sem, n_ in k.ops[ek]:
                    for (nm, val) in waits:
                        e.wait_ge(handles[nm], val)
                    instr = fn(e, handles)
                    if n_:
                        instr.then_inc(handles[sem], n_)
                if tail:
                    tail(e)
            return body

        block.tensor(emit_for("te"))
        block.scalar(emit_for("sc"))
        block.vector(emit_for("ve"))
        block.gpsimd(emit_for("gp"))
        block.sync(emit_for("sy", tail=lambda e: e.wait_ge(handles["st"],
                                                           k.cnt["st"])))

    global _LAST_SCHED
    _LAST_SCHED = k
    return nc


_NC_CACHE = {}


def _get_nc(debug=False):
    if debug not in _NC_CACHE:
        _NC_CACHE[debug] = _build_program(debug)
    return _NC_CACHE[debug]


import os as _os
_NONCE = [int.from_bytes(_os.urandom(4), "little") | 1]


def kernel(L0_attn_x, L0_attn_Wq, L0_attn_Wk, L0_attn_Wv, L0_attn_Wu,
           L0_attn_Wo, L0_ffn_w_gate, L0_ffn_w_up, L0_ffn_w_down,
           _debug=False):
    x = np.ascontiguousarray(np.asarray(L0_attn_x, dtype=np.float32)[0])
    wq = np.ascontiguousarray(np.asarray(L0_attn_Wq, dtype=np.float32))
    wk = np.ascontiguousarray(np.asarray(L0_attn_Wk, dtype=np.float32))
    wv = np.ascontiguousarray(np.asarray(L0_attn_Wv, dtype=np.float32))
    wu = np.ascontiguousarray(np.asarray(L0_attn_Wu, dtype=np.float32))
    wo = np.ascontiguousarray(np.asarray(L0_attn_Wo, dtype=np.float32))
    wg = np.ascontiguousarray(np.asarray(L0_ffn_w_gate, dtype=np.float32))
    wup = np.ascontiguousarray(np.asarray(L0_ffn_w_up, dtype=np.float32))
    wd = np.ascontiguousarray(np.asarray(L0_ffn_w_down, dtype=np.float32))

    nc = _get_nc(_debug)
    in_maps = []
    slices = []
    for r in range(NCORE):
        p, half = r // 2, r % 2
        s0 = half * HLF + p * S
        slices.append(s0)
        rest = np.concatenate([x[:s0], x[s0 + S:]], axis=0)
        xh_r = np.ascontiguousarray(np.concatenate([x[s0:s0 + S], rest], axis=0))
        in_maps.append({
            "xh": xh_r,
            "wq": wq, "wk": wk, "wv": wv, "wu": wu, "wo": wo,
            "wgf": wg, "wuf": wup, "wdf": wd,
        })
    res = run_bass_kernel_spmd(nc, in_maps, core_ids=list(range(NCORE)))
    out = np.empty((L, D), dtype=np.float32)
    for r in range(NCORE):
        out[slices[r]: slices[r] + S] = res.results[r]["out"]
    if _debug:
        return out.reshape(1, L, D), res
    return out.reshape(1, L, D)
